# version 06 (final)
# baseline (speedup 1.0000x reference)
import math
from contextlib import ExitStack
import numpy as np
import concourse.bass as bass
import concourse.mybir as mybir
from concourse.bass_utils import run_bass_kernel_spmd

F32 = mybir.dt.float32
BF16 = mybir.dt.bfloat16
I32 = mybir.dt.int32
AF = mybir.ActivationFunctionType
ALU = mybir.AluOpType
AX = mybir.AxisListType.X

NCORES = 8
S, E, H, D = 8192, 2048, 16, 128
TOK = S // NCORES
NT = TOK // 128
NCH = S // 128
RSQD = 1.0 / math.sqrt(D)
EPS = 1e-5
NEG = -1.0e9
GATHER = False
DEBUG = False
NCHUNK = 1 if GATHER else 8
SLOPES = [2.0 ** (-(h + 1) / 2.0) for h in range(H)]
BAND = [128.0 / s for s in SLOPES]

KT_LO = [[max(0, -(-(1024 * m - 127 - int(BAND[h])) // 128)) for m in range(8)]
         for h in range(H)]


def _mk_units():
    units = []
    for h in range(H):
        for kt in range(NCH):
            m_lo = max(0, -(-(kt - 7) // 8))
            m_hi = -1
            for m in range(8):
                if KT_LO[h][m] <= kt:
                    m_hi = m
            if m_hi < m_lo:
                continue
            units.append((h, kt, m_lo, m_hi))
    return units


UNITS = _mk_units()
NU = len(UNITS)
assert NU == H * NCH, NU

KV_RANK_ELEMS = 4096 * 1024
KV_V_OFF = 2048 * 1024


def _host_prep(inputs):
    x = np.asarray(inputs["L0_attn_x"])[0]
    Wq, Wk, Wv, Wo = (np.asarray(inputs[f"L0_attn_W{t}"]) for t in "qkvo")
    bq, bk, bv, bo = (np.asarray(inputs[f"L0_attn_b{t}"]) for t in "qkvo")
    gw = np.asarray(inputs["L0_ffn_gate_w"])
    gs = np.asarray(inputs["L0_ffn_gate_scale"])
    uw = np.asarray(inputs["L0_ffn_up_w"])
    us = np.asarray(inputs["L0_ffn_up_scale"])
    dw = np.asarray(inputs["L0_ffn_down_w"])
    ds = np.asarray(inputs["L0_ffn_down_scale"])

    def qk_slabs(W):
        return np.ascontiguousarray(
            W.reshape(16, 128, 16, 128).transpose(0, 3, 2, 1)).reshape(16, 128, 2048)

    wq_s, wk_s = qk_slabs(Wq), qk_slabs(Wk)
    wvt_s = np.ascontiguousarray(Wv.T.reshape(16, 128, 2048))
    wo_s = np.ascontiguousarray(Wo.T.reshape(16, 128, 2048))
    gw_s = np.ascontiguousarray(
        gw.reshape(64, 128, 16, 128).transpose(0, 3, 2, 1)).reshape(64, 128, 2048)
    uw_s = np.ascontiguousarray(
        uw.reshape(64, 128, 16, 128).transpose(0, 3, 2, 1)).reshape(64, 128, 2048)
    dw_s = np.ascontiguousarray(
        dw.reshape(16, 128, 64, 128).transpose(0, 3, 2, 1)).reshape(16, 128, 8192)
    gs_t = np.ascontiguousarray(gs.reshape(64, 128).T)
    us_t = np.ascontiguousarray(us.reshape(64, 128).T)
    ds_t = np.ascontiguousarray(ds.reshape(16, 128).T)
    bq_t = np.ascontiguousarray(bq.reshape(16, 128).T)
    bk_t = np.ascontiguousarray(bk.reshape(16, 128).T)
    bv_row = np.ascontiguousarray(bv.reshape(1, 2048))
    bo_row = np.ascontiguousarray(bo.reshape(1, 2048))

    pp = np.arange(128)[:, None]
    rr = np.arange(128)[None, :]
    in_maps = []
    for c in range(NCORES):
        xc = np.ascontiguousarray(x[c::8, :])
        xkv = np.ascontiguousarray(x[1024 * c:1024 * (c + 1), :]) if GATHER else x
        p0t = np.ascontiguousarray(np.stack(
            [(SLOPES[h] * (pp - 8.0 * rr - c)).astype(np.float32)
             for h in range(H)]))
        maskA = np.ascontiguousarray(np.stack(
            [np.where(128 * o + pp <= 8 * rr + c, 0.0, NEG).astype(np.float32)
             for o in range(8)]))
        in_maps.append({
            "xc": xc, "xkv": xkv, "wq_s": wq_s, "wk_s": wk_s,
            "wvt_s": wvt_s, "wo_s": wo_s,
            "bq_t": bq_t, "bk_t": bk_t, "bv_row": bv_row, "bo_row": bo_row,
            "gw_s": gw_s, "uw_s": uw_s, "dw_s": dw_s,
            "gs_t": gs_t, "us_t": us_t, "ds_t": ds_t,
            "p0t": p0t, "maskA": maskA,
        })
    return in_maps


def _build_nc():
    nc = bass.Bass("TRN2", target_bir_lowering=False)

    xc = nc.dram_tensor("xc", [TOK, E], F32, kind="ExternalInput")
    xkv = nc.dram_tensor("xkv", [S if not GATHER else TOK, E], F32,
                         kind="ExternalInput")
    wq_s = nc.dram_tensor("wq_s", [16, 128, 2048], F32, kind="ExternalInput")
    wk_s = nc.dram_tensor("wk_s", [16, 128, 2048], F32, kind="ExternalInput")
    wvt_s = nc.dram_tensor("wvt_s", [16, 128, 2048], F32, kind="ExternalInput")
    wo_s = nc.dram_tensor("wo_s", [16, 128, 2048], F32, kind="ExternalInput")
    bq_t = nc.dram_tensor("bq_t", [128, 16], F32, kind="ExternalInput")
    bk_t = nc.dram_tensor("bk_t", [128, 16], F32, kind="ExternalInput")
    bv_row = nc.dram_tensor("bv_row", [1, 2048], F32, kind="ExternalInput")
    bo_row = nc.dram_tensor("bo_row", [1, 2048], F32, kind="ExternalInput")
    gw_s = nc.dram_tensor("gw_s", [64, 128, 2048], I32, kind="ExternalInput")
    uw_s = nc.dram_tensor("uw_s", [64, 128, 2048], I32, kind="ExternalInput")
    dw_s = nc.dram_tensor("dw_s", [16, 128, 8192], I32, kind="ExternalInput")
    gs_t = nc.dram_tensor("gs_t", [128, 64], F32, kind="ExternalInput")
    us_t = nc.dram_tensor("us_t", [128, 64], F32, kind="ExternalInput")
    ds_t = nc.dram_tensor("ds_t", [128, 16], F32, kind="ExternalInput")
    p0t = nc.dram_tensor("p0t", [16, 128, 128], F32, kind="ExternalInput")
    maskA = nc.dram_tensor("maskA", [8, 128, 128], F32, kind="ExternalInput")
    y = nc.dram_tensor("y", [TOK, E], F32, kind="ExternalOutput")

    DK = dict(kind="ExternalOutput") if DEBUG else {}
    ln1b = nc.dram_tensor("ln1b", [TOK, E], BF16, **DK)
    lnkv = nc.dram_tensor("lnkv", [S if not GATHER else TOK, E], BF16, **DK)
    if GATHER:
        agin = nc.dram_tensor("agin", [4096, 1024], BF16)
        kv = nc.dram_tensor("agout", [8, 4096, 1024], BF16, **DK)
    else:
        kv = nc.dram_tensor("kvloc", [8, 4096, 1024], BF16, **DK)
    atnb = nc.dram_tensor("atnb", [TOK, E], BF16, **DK)
    h1scr = nc.dram_tensor("h1scr", [TOK, E], F32, **DK)
    ln2b = nc.dram_tensor("ln2b", [TOK, E], BF16, **DK)
    dscr = nc.dram_tensor("dscr", [E, TOK], BF16, **DK)

    with ExitStack() as _top:
        cn_p0 = _top.enter_context(nc.sbuf_tensor("cn_p0", [128, 16 * 128], F32))
        cn_mA = _top.enter_context(nc.sbuf_tensor("cn_mA", [128, 8 * 128], F32))
        cn_bq = _top.enter_context(nc.sbuf_tensor("cn_bq", [128, 16], F32))
        cn_bk = _top.enter_context(nc.sbuf_tensor("cn_bk", [128, 16], F32))
        cn_bvr = _top.enter_context(nc.sbuf_tensor("cn_bvr", [1, 2048], F32))
        cn_bor = _top.enter_context(nc.sbuf_tensor("cn_bor", [1, 2048], F32))
        cn_gs = _top.enter_context(nc.sbuf_tensor("cn_gs", [128, 64], F32))
        cn_us = _top.enter_context(nc.sbuf_tensor("cn_us", [128, 64], F32))
        cn_ds = _top.enter_context(nc.sbuf_tensor("cn_ds", [128, 16], F32))
        onesc = _top.enter_context(nc.sbuf_tensor("onesc", [128, 1], BF16))
        onesr_f = _top.enter_context(nc.sbuf_tensor("onesr_f", [1, 128], F32))
        epsc = _top.enter_context(nc.sbuf_tensor("epsc", [128, 1], F32))

        c_hw = _top.enter_context(nc.semaphore("c_hw"))
        cinit = _top.enter_context(nc.semaphore("cinit"))
        cbq = _top.enter_context(nc.semaphore("cbq"))
        NC_LD = 9
        with nc.Block() as blk:
            @blk.sync
            def _(sy):
                sy.dma_start(out=cn_p0[:, :], in_=bass.AP(
                    p0t, 0, [[128, 128], [128 * 128, 16], [1, 128]])).then_inc(c_hw, 16)
                sy.dma_start(out=cn_mA[:, :], in_=bass.AP(
                    maskA, 0, [[128, 128], [128 * 128, 8], [1, 128]])).then_inc(c_hw, 16)
                sy.dma_start(out=cn_bq[:, :], in_=bq_t[:, :]).then_inc(c_hw, 16)
                sy.dma_start(out=cn_bk[:, :], in_=bk_t[:, :]).then_inc(c_hw, 16)
                sy.dma_start(out=cn_bvr[:, :], in_=bv_row[:, :]).then_inc(c_hw, 16)
                sy.dma_start(out=cn_bor[:, :], in_=bo_row[:, :]).then_inc(c_hw, 16)
                sy.dma_start(out=cn_gs[:, :], in_=gs_t[:, :]).then_inc(c_hw, 16)
                sy.dma_start(out=cn_us[:, :], in_=us_t[:, :]).then_inc(c_hw, 16)
                sy.dma_start(out=cn_ds[:, :], in_=ds_t[:, :]).then_inc(c_hw, 16)

            @blk.gpsimd
            def _(g):
                g.memset(onesc[:, :], 1.0).then_inc(cinit, 1)
                g.memset(onesr_f[:, :], 1.0).then_inc(cinit, 1)
                g.memset(epsc[:, :], EPS).then_inc(cinit, 1)

            @blk.vector
            def _(v):
                v.wait_ge(c_hw, 16 * NC_LD)
                v.tensor_scalar(cn_bq[:, :], cn_bq[:, :], RSQD, None,
                                ALU.mult).then_inc(cbq, 1)

        lwb = _top.enter_context(nc.semaphore("lwb"))
        with ExitStack() as _es:
            xt = _es.enter_context(nc.sbuf_tensor("xt", [128, NT * 2048], F32))
            lnt = _es.enter_context(nc.sbuf_tensor("lnt", [128, NT * 2048], BF16))
            sq1 = _es.enter_context(nc.sbuf_tensor("sq1", [128, 2 * 2048], F32))
            st1 = _es.enter_context(nc.sbuf_tensor("st1", [128, 4 * NT], F32))
            lx = _es.enter_context(nc.semaphore("lx"))
            lred = _es.enter_context(nc.semaphore("lred"))
            lsc = _es.enter_context(nc.semaphore("lsc"))
            lnrm = _es.enter_context(nc.semaphore("lnrm"))
            lrr = _es.enter_context(nc.semaphore("lrr"))
            blk = _es.enter_context(nc.Block())

            @blk.sync
            def _(sy):
                for t in range(NT):
                    sy.dma_start(out=xt[:, 2048 * t:2048 * (t + 1)],
                                 in_=xc[128 * t:128 * (t + 1), :]).then_inc(lx, 16)
                for t in range(NT):
                    sy.wait_ge(lnrm, t + 1)
                    sy.dma_start(out=ln1b[128 * t:128 * (t + 1), :],
                                 in_=lnt[:, 2048 * t:2048 * (t + 1)]).then_inc(lwb, 16)

            @blk.vector
            def _(v):
                v.wait_ge(lx, 16 * NT)
                for t in range(NT):
                    xtile = xt[:, 2048 * t:2048 * (t + 1)]
                    v.reduce_sum(st1[:, 4 * t:4 * t + 1], xtile, axis=AX).then_inc(lred, 1)
                for t in range(NT):
                    xtile = xt[:, 2048 * t:2048 * (t + 1)]
                    v.wait_ge(lsc, 8 * (t + 1))
                    v.reciprocal(st1[:, 4 * t + 3:4 * t + 4],
                                 st1[:, 4 * t + 2:4 * t + 3]).then_inc(lrr, 1)
                    v.wait_ge(lrr, t + 1)
                    v.tensor_scalar(lnt[:, 2048 * t:2048 * (t + 1)], xtile,
                                    st1[:, 4 * t + 1:4 * t + 2],
                                    st1[:, 4 * t + 3:4 * t + 4],
                                    ALU.subtract, ALU.mult).then_inc(lnrm, 1)

            @blk.scalar
            def _(sc):
                cnt = [0]
                def st(i):
                    i.then_inc(lsc, 1)
                    cnt[0] += 1
                def wt():
                    sc.wait_ge(lsc, cnt[0])
                for t in range(NT):
                    xtile = xt[:, 2048 * t:2048 * (t + 1)]
                    sc.wait_ge(lred, t + 1)
                    c0 = st1[:, 4 * t:4 * t + 1]
                    c1 = st1[:, 4 * t + 1:4 * t + 2]
                    c2 = st1[:, 4 * t + 2:4 * t + 3]
                    st(sc.mul(c1, c0, 1.0 / 2048.0))
                    st(sc.activation(sq1[:, 2048 * (t % 2):2048 * (t % 2 + 1)], xtile,
                                     AF.Square, accum_out=c2))
                    wt()
                    st(sc.mul(c2, c2, 1.0 / 2048.0))
                    st(sc.square(c0, c1))
                    wt()
                    st(sc.mul(c0, c0, -1.0))
                    wt()
                    st(sc.add(c2, c2, c0))
                    wt()
                    st(sc.add(c2, c2, epsc[:, 0:1]))
                    wt()
                    st(sc.sqrt(c2, c2))
                assert cnt[0] == 8 * NT

        qev = _top.enter_context(nc.semaphore("qev"))
        _qs = ExitStack()
        qfm = _qs.enter_context(nc.sbuf_tensor("qfm", [128, 16 * 1024], BF16))
        with ExitStack() as _es:
            lfm = _es.enter_context(nc.sbuf_tensor("lfm", [128, 16 * 1024], BF16))
            wqb = _es.enter_context(nc.sbuf_tensor("wqb", [128, 2 * 2048], BF16))
            tfo = _es.enter_context(nc.semaphore("tfo"))
            wql = _es.enter_context(nc.semaphore("wql"))
            mmq = _es.enter_context(nc.semaphore("mmq"))
            pQ0 = _es.enter_context(nc.psum_tensor("pQ0", [128, 512], F32))
            pQ1 = _es.enter_context(nc.psum_tensor("pQ1", [128, 512], F32))
            blk = _es.enter_context(nc.Block())
            PQ = [pQ0, pQ1]

            @blk.sync
            def _(sy):
                sy.wait_ge(lwb, 16 * NT)
                for ec in range(16):
                    sy.dma_start_transpose(
                        lfm[:, 1024 * ec:1024 * (ec + 1)],
                        ln1b[:, 128 * ec:128 * (ec + 1)]).then_inc(tfo, 16)

            @blk.gpsimd
            def _(g):
                g.wait_ge(cinit, 3)
                g.wait_ge(lwb, 16 * NT)
                for h in range(16):
                    if h >= 2:
                        g.wait_ge(mmq, 2 * h - 2)
                    g.dma_start(out=wqb[:, 2048 * (h % 2):2048 * (h % 2 + 1)],
                                in_=wq_s[h, :, :]).then_inc(wql, 16)

            @blk.tensor
            def _(te):
                te.wait_ge(tfo, 16 * 16)
                for h in range(16):
                    te.wait_ge(wql, 16 * (h + 1))
                    for sub in range(2):
                        g = 2 * h + sub
                        if g >= 2:
                            te.wait_ge(qev, g - 1)
                        for ec in range(16):
                            mm = te.matmul(
                                PQ[g % 2][:, :],
                                wqb[:, 2048 * (h % 2) + 128 * ec:
                                    2048 * (h % 2) + 128 * (ec + 1)],
                                lfm[:, 1024 * ec + 512 * sub:1024 * ec + 512 * (sub + 1)],
                                start=(ec == 0), stop=(ec == 15))
                        mm.then_inc(mmq, 1)

            @blk.scalar
            def _(sc):
                sc.wait_ge(cbq, 1)
                for h in range(16):
                    for sub in range(2):
                        g = 2 * h + sub
                        sc.wait_ge(mmq, g + 1)
                        sc.activation(qfm[:, 1024 * h + 512 * sub:1024 * h + 512 * (sub + 1)],
                                      PQ[g % 2][:, :], AF.Identity,
                                      bias=cn_bq[:, h:h + 1],
                                      scale=RSQD).then_inc(qev, 1)

        kvwk = _top.enter_context(nc.semaphore("kvwk"))
        kvwv = _top.enter_context(nc.semaphore("kvwv"))
        agc = _top.enter_context(nc.semaphore("agc"))
        NKG = 32 * NCHUNK
        NVG = 32 * NCHUNK
        with ExitStack() as _es:
            wvtb = _es.enter_context(nc.sbuf_tensor("wvtb", [128, 16 * 2048], BF16))
            lfmk = _es.enter_context(nc.sbuf_tensor("lfmk", [128, 16 * 1024], BF16))
            wkb = _es.enter_context(nc.sbuf_tensor("wkb", [128, 2 * 2048], BF16))
            xt2 = _es.enter_context(nc.sbuf_tensor("xt2", [128, 2 * 2048], F32))
            lnt2 = _es.enter_context(nc.sbuf_tensor("lnt2", [128, 2 * 2048], BF16))
            sq2 = _es.enter_context(nc.sbuf_tensor("sq2", [128, 2048], F32))
            st2 = _es.enter_context(nc.sbuf_tensor("st2", [128, 8], F32))
            kev = _es.enter_context(nc.sbuf_tensor("kev", [128, 2 * 512], BF16))
            vev = _es.enter_context(nc.sbuf_tensor("vev", [128, 2 * 512], BF16))
            xls = _es.enter_context(nc.semaphore("xls"))
            lred2 = _es.enter_context(nc.semaphore("lred2"))
            lsc2 = _es.enter_context(nc.semaphore("lsc2"))
            lnrm2 = _es.enter_context(nc.semaphore("lnrm2"))
            lrr2 = _es.enter_context(nc.semaphore("lrr2"))
            lwb2 = _es.enter_context(nc.semaphore("lwb2"))
            tfm = _es.enter_context(nc.semaphore("tfm"))
            wvl = _es.enter_context(nc.semaphore("wvl"))
            wkl = _es.enter_context(nc.semaphore("wkl"))
            mmk = _es.enter_context(nc.semaphore("mmk"))
            kevd = _es.enter_context(nc.semaphore("kevd"))
            mmv = _es.enter_context(nc.semaphore("mmv"))
            vevd = _es.enter_context(nc.semaphore("vevd"))
            pK0 = _es.enter_context(nc.psum_tensor("pK0", [128, 512], F32))
            pK1 = _es.enter_context(nc.psum_tensor("pK1", [128, 512], F32))
            pV0 = _es.enter_context(nc.psum_tensor("pV0", [128, 512], F32))
            pV1 = _es.enter_context(nc.psum_tensor("pV1", [128, 512], F32))
            blk = _es.enter_context(nc.Block())
            PK = [pK0, pK1]
            PV = [pV0, pV1]

            @blk.sync
            def _(sy):
                sy.wait_ge(qev, 32)
                for ch in range(NCHUNK):
                    for t in range(8):
                        T = 8 * ch + t
                        if T >= 2:
                            sy.wait_ge(lnrm2, T - 1)
                        sy.dma_start(out=xt2[:, 2048 * (T % 2):2048 * (T % 2 + 1)],
                                     in_=xkv[128 * T:128 * (T + 1), :]).then_inc(xls, 16)
                        sy.wait_ge(lnrm2, T + 1)
                        sy.dma_start(out=lnkv[128 * T:128 * (T + 1), :],
                                     in_=lnt2[:, 2048 * (T % 2):2048 * (T % 2 + 1)]
                                     ).then_inc(lwb2, 16)
                    sy.wait_ge(lwb2, 16 * 8 * (ch + 1))
                    if ch >= 1:
                        sy.wait_ge(mmk, 32 * ch)
                        sy.wait_ge(mmv, 32 * ch)
                    for ec in range(16):
                        sy.dma_start_transpose(
                            lfmk[:, 1024 * ec:1024 * (ec + 1)],
                            lnkv[1024 * ch:1024 * (ch + 1),
                                 128 * ec:128 * (ec + 1)]).then_inc(tfm, 16)
                    rr = ch if not GATHER else 0
                    for h in range(16):
                        for sub in range(2):
                            g = 32 * ch + 2 * h + sub
                            sy.wait_ge(kevd, g + 1)
                            dst = bass.AP(kv, rr * KV_RANK_ELEMS + h * 1024 + 512 * sub,
                                          [[16384, 128], [1, 512]])
                            sy.dma_start(out=dst,
                                         in_=kev[:, 512 * (g % 2):512 * (g % 2 + 1)]
                                         ).then_inc(kvwk, 16)
                    for tt in range(8):
                        for fc in range(4):
                            gv = 32 * ch + 4 * tt + fc
                            sy.wait_ge(vevd, gv + 1)
                            dst = bass.AP(kv, rr * KV_RANK_ELEMS + KV_V_OFF
                                          + (128 * tt) * 2048 + 512 * fc,
                                          [[2048, 128], [1, 512]])
                            sy.dma_start(out=dst,
                                         in_=vev[:, 512 * (gv % 2):512 * (gv % 2 + 1)]
                                         ).then_inc(kvwv, 16)

            @blk.vector
            def _(v):
                for ch in range(NCHUNK):
                    for t in range(8):
                        T = 8 * ch + t
                        xtile = xt2[:, 2048 * (T % 2):2048 * (T % 2 + 1)]
                        c0 = st2[:, 4 * (T % 2):4 * (T % 2) + 1]
                        c1 = st2[:, 4 * (T % 2) + 1:4 * (T % 2) + 2]
                        c2 = st2[:, 4 * (T % 2) + 2:4 * (T % 2) + 3]
                        c3 = st2[:, 4 * (T % 2) + 3:4 * (T % 2) + 4]
                        v.wait_ge(xls, 16 * (T + 1))
                        v.reduce_sum(c0, xtile, axis=AX).then_inc(lred2, 1)
                        v.wait_ge(lsc2, 8 * (T + 1))
                        v.reciprocal(c3, c2).then_inc(lrr2, 1)
                        v.wait_ge(lrr2, T + 1)
                        if T >= 2:
                            v.wait_ge(lwb2, 16 * (T - 1))
                        v.tensor_scalar(lnt2[:, 2048 * (T % 2):2048 * (T % 2 + 1)],
                                        xtile, c1, c3,
                                        ALU.subtract, ALU.mult).then_inc(lnrm2, 1)

            @blk.scalar
            def _(sc):
                cnt = [0]
                def st(i):
                    i.then_inc(lsc2, 1)
                    cnt[0] += 1
                def wt():
                    sc.wait_ge(lsc2, cnt[0])
                for ch in range(NCHUNK):
                    for t in range(8):
                        T = 8 * ch + t
                        xtile = xt2[:, 2048 * (T % 2):2048 * (T % 2 + 1)]
                        sc.wait_ge(lred2, T + 1)
                        if T >= 2:
                            sc.wait_ge(lnrm2, T - 1)
                        c0 = st2[:, 4 * (T % 2):4 * (T % 2) + 1]
                        c1 = st2[:, 4 * (T % 2) + 1:4 * (T % 2) + 2]
                        c2 = st2[:, 4 * (T % 2) + 2:4 * (T % 2) + 3]
                        st(sc.mul(c1, c0, 1.0 / 2048.0))
                        st(sc.activation(sq2[:, :], xtile, AF.Square, accum_out=c2))
                        wt()
                        st(sc.mul(c2, c2, 1.0 / 2048.0))
                        st(sc.square(c0, c1))
                        wt()
                        st(sc.mul(c0, c0, -1.0))
                        wt()
                        st(sc.add(c2, c2, c0))
                        wt()
                        st(sc.add(c2, c2, epsc[:, 0:1]))
                        wt()
                        st(sc.sqrt(c2, c2))
                    for h in range(16):
                        for sub in range(2):
                            g = 32 * ch + 2 * h + sub
                            sc.wait_ge(mmk, g + 1)
                            if g >= 2:
                                sc.wait_ge(kvwk, 16 * (g - 1))
                            sc.activation(kev[:, 512 * (g % 2):512 * (g % 2 + 1)],
                                          PK[g % 2][:, :], AF.Identity,
                                          bias=cn_bk[:, h:h + 1]).then_inc(kevd, 1)
                    for tt in range(8):
                        for fc in range(4):
                            gv = 32 * ch + 4 * tt + fc
                            sc.wait_ge(mmv, gv + 1)
                            if gv >= 2:
                                sc.wait_ge(kvwv, 16 * (gv - 1))
                            sc.activation(vev[:, 512 * (gv % 2):512 * (gv % 2 + 1)],
                                          PV[gv % 2][:, :],
                                          AF.Identity).then_inc(vevd, 1)
                assert cnt[0] == 8 * 8 * NCHUNK

            @blk.gpsimd
            def _(g_):
                g_.wait_ge(qev, 32)
                for ec in range(16):
                    g_.dma_start(out=wvtb[:, 2048 * ec:2048 * (ec + 1)],
                                 in_=wvt_s[ec, :, :]).then_inc(wvl, 16)
                for ch in range(NCHUNK):
                    for h in range(16):
                        hg = 16 * ch + h
                        if hg >= 2:
                            g_.wait_ge(mmk, 2 * hg - 2)
                        g_.dma_start(out=wkb[:, 2048 * (hg % 2):2048 * (hg % 2 + 1)],
                                     in_=wk_s[h, :, :]).then_inc(wkl, 16)
                if GATHER:
                    g_.wait_ge(kvwk, 16 * NKG)
                    g_.wait_ge(kvwv, 16 * NVG)
                    g_.collective_compute(
                        "AllGather", ALU.bypass,
                        replica_groups=[list(range(NCORES))],
                        ins=[agin.ap().opt()], outs=[kv.ap().opt()],
                    ).then_inc(agc)

            @blk.tensor
            def _(te):
                te.wait_ge(wvl, 16 * 16)
                for ch in range(NCHUNK):
                    te.wait_ge(tfm, 16 * 16 * (ch + 1))
                    for h in range(16):
                        hg = 16 * ch + h
                        te.wait_ge(wkl, 16 * (hg + 1))
                        for sub in range(2):
                            g = 32 * ch + 2 * h + sub
                            if g >= 2:
                                te.wait_ge(kevd, g - 1)
                            for ec in range(16):
                                mm = te.matmul(
                                    PK[g % 2][:, :],
                                    wkb[:, 2048 * (hg % 2) + 128 * ec:
                                        2048 * (hg % 2) + 128 * (ec + 1)],
                                    lfmk[:, 1024 * ec + 512 * sub:
                                         1024 * ec + 512 * (sub + 1)],
                                    start=(ec == 0), stop=(ec == 15))
                            mm.then_inc(mmk, 1)
                    for tt in range(8):
                        for fc in range(4):
                            gv = 32 * ch + 4 * tt + fc
                            if gv >= 2:
                                te.wait_ge(vevd, gv - 1)
                            for ec in range(16):
                                te.matmul(
                                    PV[gv % 2][:, :],
                                    lfmk[:, 1024 * ec + 128 * tt:
                                         1024 * ec + 128 * (tt + 1)],
                                    wvtb[:, 2048 * ec + 512 * fc:
                                         2048 * ec + 512 * (fc + 1)],
                                    start=(ec == 0), stop=False)
                            te.matmul(PV[gv % 2][:, :], onesr_f[:, :],
                                      cn_bvr[:, 512 * fc:512 * (fc + 1)],
                                      start=False, stop=True).then_inc(mmv, 1)

        cdone = _top.enter_context(nc.semaphore("cdone"))
        awb = _top.enter_context(nc.semaphore("awb"))
        KVWK_T, KVWV_T = 16 * NKG, 16 * NVG

        def kv_gate(eng):
            if GATHER:
                eng.wait_ge(agc, 1)
            else:
                eng.wait_ge(kvwk, KVWK_T)
                eng.wait_ge(kvwv, KVWV_T)

        with ExitStack() as _es:
            attok = _es.enter_context(nc.sbuf_tensor("attok", [128, NT * 2048], BF16))
            sa = _es.enter_context(nc.sbuf_tensor("sa", [128, 2 * 1024], F32))
            ebuf = _es.enter_context(nc.sbuf_tensor("ebuf", [128, 2 * 1024], BF16))
            kbuf = _es.enter_context(nc.sbuf_tensor("kbuf", [128, 2 * 128], BF16))
            vbuf = _es.enter_context(nc.sbuf_tensor("vbuf", [128, 2 * 132], BF16))
            accs = _es.enter_context(nc.sbuf_tensor("accs", [128, 8 * 132], F32))
            dstg = _es.enter_context(nc.sbuf_tensor("dstg", [128, 16], F32))
            ssem = _es.enter_context(nc.semaphore("ssem"))
            vsem = _es.enter_context(nc.semaphore("vsem"))
            esem = _es.enter_context(nc.semaphore("esem"))
            pvs = _es.enter_context(nc.semaphore("pvs"))
            pva = _es.enter_context(nc.semaphore("pva"))
            kls = _es.enter_context(nc.semaphore("kls"))
            vls = _es.enter_context(nc.semaphore("vls"))
            vone = _es.enter_context(nc.semaphore("vone"))
            vslf = _es.enter_context(nc.semaphore("vslf"))
            sslf = _es.enter_context(nc.semaphore("sslf"))
            pS = [_es.enter_context(nc.psum_tensor(f"pS{j}", [128, 512], F32))
                  for j in range(2)]
            pPV = [[_es.enter_context(nc.psum_tensor(f"pPV{p}{j}", [128, 512], F32))
                    for j in range(3)] for p in range(2)]
            blk = _es.enter_context(nc.Block())

            def pv_slot(p, i):
                return pPV[p % 2][i // 3][:, 132 * (i % 3):132 * (i % 3) + 129]

            def acc_out(m):
                return accs[:, 132 * m:132 * m + 129]

            def acc_num(m):
                return accs[:, 132 * m:132 * m + 128]

            def acc_den(m):
                return accs[:, 132 * m + 128:132 * m + 129]

            @blk.sync
            def _(sy):
                kv_gate(sy)
                for u, (h, kt, m_lo, m_hi) in enumerate(UNITS):
                    if u >= 1:
                        sy.wait_ge(ssem, u)
                    r, tl = kt // 8, 128 * (kt % 8)
                    sy.dma_start(
                        out=kbuf[:, 128 * (u % 2):128 * (u % 2 + 1)],
                        in_=bass.AP(kv, r * KV_RANK_ELEMS + h * 1024 + tl,
                                    [[16384, 128], [1, 128]])).then_inc(kls, 16)
                sy.wait_ge(cdone, 128)
                for t in range(NT):
                    sy.dma_start(out=atnb[128 * t:128 * (t + 1), :],
                                 in_=attok[:, 2048 * t:2048 * (t + 1)]).then_inc(awb, 16)
                sy.wait_ge(awb, 16 * NT)

            @blk.gpsimd
            def _(g):
                kv_gate(g)
                g.memset(vbuf[:, 128:129], 1.0).then_inc(vone, 1)
                g.memset(vbuf[:, 260:261], 1.0).then_inc(vone, 1)
                for u, (h, kt, m_lo, m_hi) in enumerate(UNITS):
                    if u >= 1:
                        g.wait_ge(pvs, u)
                    r, tl = kt // 8, 128 * (kt % 8)
                    g.dma_start(
                        out=vbuf[:, 132 * (u % 2):132 * (u % 2) + 128],
                        in_=bass.AP(kv, r * KV_RANK_ELEMS + KV_V_OFF + tl * 2048 + 128 * h,
                                    [[2048, 128], [1, 128]])).then_inc(vls, 16)

            def emit_pv(te, p):
                h, kt, m_lo, m_hi = UNITS[p]
                te.wait_ge(esem, p + 1)
                te.wait_ge(vls, 16 * (p + 1))
                if p >= 2:
                    te.wait_ge(pva, p - 1)
                last = None
                for i, m in enumerate(range(m_lo, m_hi + 1)):
                    last = te.matmul(
                        pv_slot(p, i),
                        ebuf[:, 1024 * (p % 2) + 128 * i:1024 * (p % 2) + 128 * (i + 1)],
                        vbuf[:, 132 * (p % 2):132 * (p % 2) + 129],
                        start=True, stop=True)
                last.then_inc(pvs, 1)

            @blk.tensor
            def _(te):
                kv_gate(te)
                te.wait_ge(qev, 32)
                te.wait_ge(vone, 2)
                for u, (h, kt, m_lo, m_hi) in enumerate(UNITS):
                    te.wait_ge(kls, 16 * (u + 1))
                    if u >= 1:
                        te.wait_ge(vsem, u)
                    cols = 128 * (m_hi - m_lo + 1)
                    q0 = 1024 * h + 128 * m_lo
                    last = None
                    for j in range(2):
                        n = min(512, cols - 512 * j)
                        if n <= 0:
                            break
                        last = te.matmul(
                            pS[j][:, 0:n],
                            kbuf[:, 128 * (u % 2):128 * (u % 2 + 1)],
                            qfm[:, q0 + 512 * j:q0 + 512 * j + n],
                            start=True, stop=True)
                    last.then_inc(ssem, 1)
                    if u >= 1:
                        emit_pv(te, u - 1)
                emit_pv(te, NU - 1)

            def emit_pvadd(v, p, nvself):
                h, kt, m_lo, m_hi = UNITS[p]
                stop_unit = (kt % 8 == 7)
                v.wait_ge(pvs, p + 1)
                for i, m in enumerate(range(m_lo, m_hi + 1)):
                    if kt == KT_LO[h][m]:
                        ins = v.tensor_copy(acc_out(m), pv_slot(p, i))
                    else:
                        ins = v.tensor_tensor(acc_out(m), pv_slot(p, i),
                                              acc_out(m), ALU.add)
                    ins.then_inc(vslf, 1)
                    nvself[0] += 1
                v.wait_ge(vslf, nvself[0])
                v.sem_inc(pva, 1)
                if stop_unit:
                    m_ev = (kt - 7) // 8
                    dcol = 2 * ((8 * h + m_ev) % 8)
                    v.reciprocal(dstg[:, dcol + 1:dcol + 2],
                                 acc_den(m_ev)).then_inc(vslf, 1)
                    nvself[0] += 1
                    v.wait_ge(vslf, nvself[0])
                    v.tensor_scalar(
                        attok[:, 2048 * m_ev + 128 * h:2048 * m_ev + 128 * (h + 1)],
                        acc_num(m_ev), dstg[:, dcol + 1:dcol + 2], None,
                        ALU.mult).then_inc(vslf, 1)
                    nvself[0] += 1
                    v.wait_ge(vslf, nvself[0])
                    v.sem_inc(cdone, 1)

            @blk.vector
            def _(v):
                kv_gate(v)
                nvself = [0]
                for u, (h, kt, m_lo, m_hi) in enumerate(UNITS):
                    v.wait_ge(ssem, u + 1)
                    if u >= 2:
                        v.wait_ge(esem, u - 1)
                    for i, m in enumerate(range(m_lo, m_hi + 1)):
                        src = pS[i // 4][:, 128 * (i % 4):128 * (i % 4 + 1)]
                        dst = sa[:, 1024 * (u % 2) + 128 * i:1024 * (u % 2) + 128 * (i + 1)]
                        C = SLOPES[h] * 128.0 * (kt - 8 * m)
                        o = kt - 8 * m
                        if 0 <= o <= 7:
                            v.scalar_tensor_tensor(
                                dst, src, C, cn_p0[:, 128 * h:128 * (h + 1)],
                                ALU.add, ALU.add).then_inc(vslf, 1)
                            nvself[0] += 1
                            v.wait_ge(vslf, nvself[0])
                            v.tensor_tensor(dst, dst,
                                            cn_mA[:, 128 * o:128 * (o + 1)],
                                            ALU.add).then_inc(vslf, 1)
                        else:
                            v.scalar_tensor_tensor(
                                dst, src, C, cn_p0[:, 128 * h:128 * (h + 1)],
                                ALU.add, ALU.add).then_inc(vslf, 1)
                        nvself[0] += 1
                    v.wait_ge(vslf, nvself[0])
                    v.sem_inc(vsem, 1)
                    if u >= 1:
                        emit_pvadd(v, u - 1, nvself)
                emit_pvadd(v, NU - 1, nvself)

            @blk.scalar
            def _(sc):
                kv_gate(sc)
                nss = 0
                for u, (h, kt, m_lo, m_hi) in enumerate(UNITS):
                    sc.wait_ge(vsem, u + 1)
                    if u >= 2:
                        sc.wait_ge(pvs, u - 1)
                    for i, m in enumerate(range(m_lo, m_hi + 1)):
                        sc.activation(
                            ebuf[:, 1024 * (u % 2) + 128 * i:1024 * (u % 2) + 128 * (i + 1)],
                            sa[:, 1024 * (u % 2) + 128 * i:1024 * (u % 2) + 128 * (i + 1)],
                            AF.Exp).then_inc(sslf, 1)
                        nss += 1
                    sc.wait_ge(sslf, nss)
                    sc.sem_inc(esem, 1)

        _qs.close()
        h1d0 = _top.enter_context(nc.semaphore("h1d0"))
        h1d1 = _top.enter_context(nc.semaphore("h1d1"))
        l2d0 = _top.enter_context(nc.semaphore("l2d0"))
        l2d1 = _top.enter_context(nc.semaphore("l2d1"))
        mo = _top.enter_context(nc.semaphore("mo"))
        with ExitStack() as _es:
            afm = _es.enter_context(nc.sbuf_tensor("afm", [128, 16 * 1024], BF16))
            wofull = _es.enter_context(nc.sbuf_tensor("wofull", [128, 16 * 2048], BF16))
            x2 = _es.enter_context(nc.sbuf_tensor("x2", [128, 2 * 2048], F32))
            h1t = _es.enter_context(nc.sbuf_tensor("h1t", [128, 2 * 2048], F32))
            l2t = _es.enter_context(nc.sbuf_tensor("l2t", [128, 2 * 2048], BF16))
            sq6 = _es.enter_context(nc.sbuf_tensor("sq6", [128, 2048], F32))
            st6 = _es.enter_context(nc.sbuf_tensor("st6", [128, 4 * NT], F32))
            atr = _es.enter_context(nc.semaphore("atr"))
            wol = _es.enter_context(nc.semaphore("wol"))
            x2l0 = _es.enter_context(nc.semaphore("x2l0"))
            x2l1 = _es.enter_context(nc.semaphore("x2l1"))
            oev = _es.enter_context(nc.semaphore("oev"))
            r6 = _es.enter_context(nc.semaphore("r6"))
            s6 = _es.enter_context(nc.semaphore("s6"))
            n6 = _es.enter_context(nc.semaphore("n6"))
            rr6 = _es.enter_context(nc.semaphore("rr6"))
            po = [_es.enter_context(nc.psum_tensor(f"po{j}", [128, 512], F32))
                  for j in range(4)]
            blk = _es.enter_context(nc.Block())

            @blk.sync
            def _(sy):
                for ec in range(16):
                    sy.dma_start_transpose(
                        afm[:, 1024 * ec:1024 * (ec + 1)],
                        atnb[:, 128 * ec:128 * (ec + 1)]).then_inc(atr, 16)
                for t in range(NT):
                    if t >= 2:
                        sy.wait_ge(oev, 4 * (t - 1))
                    sy.dma_start(out=x2[:, 2048 * (t % 2):2048 * (t % 2 + 1)],
                                 in_=xc[128 * t:128 * (t + 1), :]
                                 ).then_inc([x2l0, x2l1][t % 2], 16)

            @blk.gpsimd
            def _(g):
                g.wait_ge(awb, 16 * NT)
                for ec in range(16):
                    g.dma_start(out=wofull[:, 2048 * ec:2048 * (ec + 1)],
                                in_=wo_s[ec, :, :]).then_inc(wol, 16)
                for t in range(NT):
                    g.wait_ge(oev, 4 * (t + 1))
                    g.dma_start(out=h1scr[128 * t:128 * (t + 1), :],
                                in_=h1t[:, 2048 * (t % 2):2048 * (t % 2 + 1)]
                                ).then_inc([h1d0, h1d1][t % 2], 16)
                    g.wait_ge(n6, t + 1)
                    g.dma_start(out=ln2b[128 * t:128 * (t + 1), :],
                                in_=l2t[:, 2048 * (t % 2):2048 * (t % 2 + 1)]
                                ).then_inc([l2d0, l2d1][t % 2], 16)
                g.wait_ge(h1d0, 16 * (NT // 2))
                g.wait_ge(h1d1, 16 * (NT // 2))
                g.wait_ge(l2d0, 16 * (NT // 2))
                g.wait_ge(l2d1, 16 * (NT // 2))

            @blk.tensor
            def _(te):
                te.wait_ge(atr, 16 * 16)
                te.wait_ge(wol, 16 * 16)
                for t in range(NT):
                    if t >= 1:
                        te.wait_ge(oev, 4 * t)
                    for ec in range(16):
                        for e4 in range(4):
                            te.matmul(po[e4][:, :],
                                      afm[:, 1024 * ec + 128 * t:1024 * ec + 128 * (t + 1)],
                                      wofull[:, 2048 * ec + 512 * e4:
                                             2048 * ec + 512 * (e4 + 1)],
                                      start=(ec == 0), stop=False)
                    for e4 in range(4):
                        te.matmul(po[e4][:, :], onesr_f[:, :],
                                  cn_bor[:, 512 * e4:512 * (e4 + 1)],
                                  start=False, stop=True).then_inc(mo, 1)

            @blk.vector
            def _(v):
                for t in range(NT):
                    v.wait_ge(mo, 4 * (t + 1))
                    v.wait_ge([x2l0, x2l1][t % 2], 16 * (t // 2 + 1))
                    if t >= 2:
                        v.wait_ge([h1d0, h1d1][t % 2], 16 * (t // 2))
                        v.wait_ge([l2d0, l2d1][t % 2], 16 * (t // 2))
                    h1 = h1t[:, 2048 * (t % 2):2048 * (t % 2 + 1)]
                    for e4 in range(4):
                        v.tensor_tensor(h1[:, 512 * e4:512 * (e4 + 1)], po[e4][:, :],
                                        x2[:, 2048 * (t % 2) + 512 * e4:
                                           2048 * (t % 2) + 512 * (e4 + 1)],
                                        ALU.add).then_inc(oev, 1)
                    v.wait_ge(oev, 4 * (t + 1))
                    v.reduce_sum(st6[:, 4 * t:4 * t + 1], h1, axis=AX).then_inc(r6, 1)
                    v.wait_ge(s6, 8 * (t + 1))
                    v.reciprocal(st6[:, 4 * t + 3:4 * t + 4],
                                 st6[:, 4 * t + 2:4 * t + 3]).then_inc(rr6, 1)
                    v.wait_ge(rr6, t + 1)
                    v.tensor_scalar(l2t[:, 2048 * (t % 2):2048 * (t % 2 + 1)], h1,
                                    st6[:, 4 * t + 1:4 * t + 2],
                                    st6[:, 4 * t + 3:4 * t + 4],
                                    ALU.subtract, ALU.mult).then_inc(n6, 1)

            @blk.scalar
            def _(sc):
                cnt = [0]
                def st(i):
                    i.then_inc(s6, 1)
                    cnt[0] += 1
                def wt():
                    sc.wait_ge(s6, cnt[0])
                for t in range(NT):
                    sc.wait_ge(r6, t + 1)
                    h1 = h1t[:, 2048 * (t % 2):2048 * (t % 2 + 1)]
                    c0 = st6[:, 4 * t:4 * t + 1]
                    c1 = st6[:, 4 * t + 1:4 * t + 2]
                    c2 = st6[:, 4 * t + 2:4 * t + 3]
                    st(sc.mul(c1, c0, 1.0 / 2048.0))
                    st(sc.activation(sq6[:, :], h1, AF.Square, accum_out=c2))
                    wt()
                    st(sc.mul(c2, c2, 1.0 / 2048.0))
                    st(sc.square(c0, c1))
                    wt()
                    st(sc.mul(c0, c0, -1.0))
                    wt()
                    st(sc.add(c2, c2, c0))
                    wt()
                    st(sc.add(c2, c2, epsc[:, 0:1]))
                    wt()
                    st(sc.sqrt(c2, c2))
                assert cnt[0] == 8 * NT

        ddn0 = _top.enter_context(nc.semaphore("ddn0"))
        ddn1 = _top.enter_context(nc.semaphore("ddn1"))
        with ExitStack() as _es:
            lfm2 = _es.enter_context(nc.sbuf_tensor("lfm2", [128, 16 * 1024], BF16))
            hact = _es.enter_context(nc.sbuf_tensor("hact", [128, 64 * 512], BF16))
            gub = _es.enter_context(nc.sbuf_tensor("gub", [128, 4 * 2048], BF16))
            dwb = _es.enter_context(nc.sbuf_tensor("dwb", [128, 2 * 8192], BF16))
            gtmp = _es.enter_context(nc.sbuf_tensor("gtmp", [128, 2 * 512], F32))
            gsil = _es.enter_context(nc.sbuf_tensor("gsil", [128, 2 * 512], BF16))
            utmp = _es.enter_context(nc.sbuf_tensor("utmp", [128, 2 * 512], BF16))
            crow = _es.enter_context(nc.sbuf_tensor("crow", [1, 512], F32))
            cdrow = _es.enter_context(nc.sbuf_tensor("cdrow", [1, 512], F32))
            dno = _es.enter_context(nc.sbuf_tensor("dno", [128, 2 * 512], BF16))
            tf2 = _es.enter_context(nc.semaphore("tf2"))
            gwl0 = _es.enter_context(nc.semaphore("gwl0"))
            gwl1 = _es.enter_context(nc.semaphore("gwl1"))
            dwl0 = _es.enter_context(nc.semaphore("dwl0"))
            dwl1 = _es.enter_context(nc.semaphore("dwl1"))
            mcr = _es.enter_context(nc.semaphore("mcr"))
            ecr = _es.enter_context(nc.semaphore("ecr"))
            mgu = _es.enter_context(nc.semaphore("mgu"))
            egu = _es.enter_context(nc.semaphore("egu"))
            sgu = _es.enter_context(nc.semaphore("sgu"))
            hgu = _es.enter_context(nc.semaphore("hgu"))
            mcd = _es.enter_context(nc.semaphore("mcd"))
            ecd = _es.enter_context(nc.semaphore("ecd"))
            mdn = _es.enter_context(nc.semaphore("mdn"))
            edn = _es.enter_context(nc.semaphore("edn"))
            pg = [_es.enter_context(nc.psum_tensor(f"pg{j}", [128, 512], F32))
                  for j in range(2)]
            pu = [_es.enter_context(nc.psum_tensor(f"pu{j}", [128, 512], F32))
                  for j in range(2)]
            pcr = _es.enter_context(nc.psum_tensor("pcr", [128, 512], F32))
            pdn = [_es.enter_context(nc.psum_tensor(f"pdn{j}", [128, 512], F32))
                   for j in range(2)]
            blk = _es.enter_context(nc.Block())

            @blk.sync
            def _(sy):
                sy.wait_ge(l2d0, 16 * (NT // 2))
                sy.wait_ge(l2d1, 16 * (NT // 2))
                sy.wait_ge(mo, 4 * NT)
                for ec in range(16):
                    sy.dma_start_transpose(
                        lfm2[:, 1024 * ec:1024 * (ec + 1)],
                        ln2b[:, 128 * ec:128 * (ec + 1)]).then_inc(tf2, 16)
                for tb in range(2):
                    for hc in range(16):
                        i6 = 16 * tb + hc
                        sy.wait_ge(edn, i6 + 1)
                        sy.dma_start(out=dscr[128 * hc:128 * (hc + 1),
                                              512 * tb:512 * (tb + 1)],
                                     in_=dno[:, 512 * (i6 % 2):512 * (i6 % 2 + 1)]
                                     ).then_inc([ddn0, ddn1][i6 % 2], 16)
                sy.wait_ge(ddn0, 16 * 16)
                sy.wait_ge(ddn1, 16 * 16)

            @blk.gpsimd
            def _(g):
                for tb in range(2):
                    for ic in range(64):
                        si = 64 * tb + ic
                        if si >= 2:
                            g.wait_ge(mgu, 2 * si - 2)
                        g.dma_start(out=gub[:, 2048 * (2 * (si % 2)):
                                            2048 * (2 * (si % 2) + 1)],
                                    in_=gw_s[ic, :, :]).then_inc([gwl0, gwl1][si % 2], 16)
                        g.dma_start(out=gub[:, 2048 * (2 * (si % 2) + 1):
                                            2048 * (2 * (si % 2) + 2)],
                                    in_=uw_s[ic, :, :]).then_inc([gwl0, gwl1][si % 2], 16)
                    for hc in range(16):
                        di = 16 * tb + hc
                        if di >= 2:
                            g.wait_ge(mdn, di - 1)
                        g.dma_start(out=dwb[:, 8192 * (di % 2):8192 * (di % 2 + 1)],
                                    in_=dw_s[hc, :, :]).then_inc([dwl0, dwl1][di % 2], 16)

            @blk.tensor
            def _(te):
                te.wait_ge(tf2, 16 * 16)
                for tb in range(2):
                    if tb >= 1:
                        te.wait_ge(ecd, tb)
                    for ec in range(16):
                        mm = te.matmul(pcr[0:1, 0:512], onesc[:, :],
                                       lfm2[:, 1024 * ec + 512 * tb:
                                            1024 * ec + 512 * (tb + 1)],
                                       start=(ec == 0), stop=(ec == 15))
                    mm.then_inc(mcr, 1)
                    te.wait_ge(ecr, tb + 1)
                    for ic in range(64):
                        si = 64 * tb + ic
                        te.wait_ge([gwl0, gwl1][si % 2], 32 * (si // 2 + 1))
                        if si >= 2:
                            te.wait_ge(hgu, si - 1)
                        gsl = gub[:, 2048 * (2 * (si % 2)):2048 * (2 * (si % 2) + 1)]
                        usl = gub[:, 2048 * (2 * (si % 2) + 1):2048 * (2 * (si % 2) + 2)]
                        for hc in range(16):
                            te.matmul(pg[si % 2][:, :],
                                      gsl[:, 128 * hc:128 * (hc + 1)],
                                      lfm2[:, 1024 * hc + 512 * tb:
                                           1024 * hc + 512 * (tb + 1)],
                                      start=(hc == 0), stop=False)
                        te.matmul(pg[si % 2][:, :], onesr_f[:, :], crow[:, :],
                                  start=False, stop=True).then_inc(mgu, 1)
                        for hc in range(16):
                            te.matmul(pu[si % 2][:, :],
                                      usl[:, 128 * hc:128 * (hc + 1)],
                                      lfm2[:, 1024 * hc + 512 * tb:
                                           1024 * hc + 512 * (tb + 1)],
                                      start=(hc == 0), stop=False)
                        te.matmul(pu[si % 2][:, :], onesr_f[:, :], crow[:, :],
                                  start=False, stop=True).then_inc(mgu, 1)
                    te.wait_ge(hgu, 64 * (tb + 1))
                    for ic in range(64):
                        mm = te.matmul(pcr[0:1, 0:512], onesc[:, :],
                                       hact[:, 512 * ic:512 * (ic + 1)],
                                       start=(ic == 0), stop=(ic == 63))
                    mm.then_inc(mcd, 1)
                    te.wait_ge(ecd, tb + 1)
                    for hc in range(16):
                        di = 16 * tb + hc
                        te.wait_ge([dwl0, dwl1][di % 2], 16 * (di // 2 + 1))
                        if di >= 2:
                            te.wait_ge(edn, di - 1)
                        for ic in range(64):
                            te.matmul(pdn[di % 2][:, :],
                                      dwb[:, 8192 * (di % 2) + 128 * ic:
                                          8192 * (di % 2) + 128 * (ic + 1)],
                                      hact[:, 512 * ic:512 * (ic + 1)],
                                      start=(ic == 0), stop=False)
                        te.matmul(pdn[di % 2][:, :], onesr_f[:, :], cdrow[:, :],
                                  start=False, stop=True).then_inc(mdn, 1)

            @blk.scalar
            def _(sc):
                for tb in range(2):
                    sc.wait_ge(mcr, tb + 1)
                    if tb >= 1:
                        sc.wait_ge(mgu, 128 * tb)
                    sc.mul(crow[:, :], pcr[0:1, 0:512], -128.0).then_inc(ecr, 1)
                    for ic in range(64):
                        si = 64 * tb + ic
                        sc.wait_ge(egu, 2 * si + 1)
                        if si >= 2:
                            sc.wait_ge(hgu, si - 1)
                        sc.activation(gsil[:, 512 * (si % 2):512 * (si % 2 + 1)],
                                      gtmp[:, 512 * (si % 2):512 * (si % 2 + 1)],
                                      AF.Silu).then_inc(sgu, 1)
                    sc.wait_ge(mcd, tb + 1)
                    if tb >= 1:
                        sc.wait_ge(mdn, 16 * tb)
                    sc.mul(cdrow[:, :], pcr[0:1, 0:512], -128.0).then_inc(ecd, 1)

            @blk.vector
            def _(v):
                for tb in range(2):
                    for ic in range(64):
                        si = 64 * tb + ic
                        v.wait_ge(mgu, 2 * si + 1)
                        if si >= 2:
                            v.wait_ge(sgu, si - 1)
                        v.tensor_scalar(gtmp[:, 512 * (si % 2):512 * (si % 2 + 1)],
                                        pg[si % 2][:, :], cn_gs[:, ic:ic + 1], None,
                                        ALU.mult).then_inc(egu, 1)
                        v.wait_ge(mgu, 2 * si + 2)
                        v.tensor_scalar(utmp[:, 512 * (si % 2):512 * (si % 2 + 1)],
                                        pu[si % 2][:, :], cn_us[:, ic:ic + 1], None,
                                        ALU.mult).then_inc(egu, 1)
                        v.wait_ge(sgu, si + 1)
                        v.wait_ge(egu, 2 * si + 2)
                        v.tensor_tensor(hact[:, 512 * ic:512 * (ic + 1)],
                                        gsil[:, 512 * (si % 2):512 * (si % 2 + 1)],
                                        utmp[:, 512 * (si % 2):512 * (si % 2 + 1)],
                                        ALU.mult).then_inc(hgu, 1)
                    for hc in range(16):
                        di = 16 * tb + hc
                        v.wait_ge(mdn, di + 1)
                        if di >= 2:
                            v.wait_ge([ddn0, ddn1][di % 2], 16 * (di // 2))
                        v.tensor_scalar(dno[:, 512 * (di % 2):512 * (di % 2 + 1)],
                                        pdn[di % 2][:, :], cn_ds[:, hc:hc + 1], None,
                                        ALU.mult).then_inc(edn, 1)

        with ExitStack() as _es:
            dnt = _es.enter_context(nc.sbuf_tensor("dnt", [128, 2 * 2048], BF16))
            h1r = _es.enter_context(nc.sbuf_tensor("h1r", [128, 2 * 2048], F32))
            yt = _es.enter_context(nc.sbuf_tensor("yt", [128, 2 * 2048], F32))
            dtr0 = _es.enter_context(nc.semaphore("dtr0"))
            dtr1 = _es.enter_context(nc.semaphore("dtr1"))
            h1l0 = _es.enter_context(nc.semaphore("h1l0"))
            h1l1 = _es.enter_context(nc.semaphore("h1l1"))
            yv = _es.enter_context(nc.semaphore("yv"))
            yd0 = _es.enter_context(nc.semaphore("yd0"))
            yd1 = _es.enter_context(nc.semaphore("yd1"))
            blk = _es.enter_context(nc.Block())

            @blk.sync
            def _(sy):
                for t in range(NT):
                    if t >= 2:
                        sy.wait_ge(yv, t - 1)
                    sy.dma_start_transpose(
                        dnt[:, 2048 * (t % 2):2048 * (t % 2 + 1)],
                        dscr[:, 128 * t:128 * (t + 1)]).then_inc([dtr0, dtr1][t % 2], 16)
                    sy.dma_start(out=h1r[:, 2048 * (t % 2):2048 * (t % 2 + 1)],
                                 in_=h1scr[128 * t:128 * (t + 1), :]
                                 ).then_inc([h1l0, h1l1][t % 2], 16)

            @blk.vector
            def _(v):
                for t in range(NT):
                    v.wait_ge([dtr0, dtr1][t % 2], 16 * (t // 2 + 1))
                    v.wait_ge([h1l0, h1l1][t % 2], 16 * (t // 2 + 1))
                    if t >= 2:
                        v.wait_ge([yd0, yd1][t % 2], 16 * (t // 2))
                    v.tensor_tensor(yt[:, 2048 * (t % 2):2048 * (t % 2 + 1)],
                                    h1r[:, 2048 * (t % 2):2048 * (t % 2 + 1)],
                                    dnt[:, 2048 * (t % 2):2048 * (t % 2 + 1)],
                                    ALU.add).then_inc(yv, 1)

            @blk.gpsimd
            def _(g):
                for t in range(NT):
                    g.wait_ge(yv, t + 1)
                    g.dma_start(out=y[128 * t:128 * (t + 1), :],
                                in_=yt[:, 2048 * (t % 2):2048 * (t % 2 + 1)]
                                ).then_inc([yd0, yd1][t % 2], 16)
                g.wait_ge(yd0, 16 * (NT // 2))
                g.wait_ge(yd1, 16 * (NT // 2))

    return nc


def kernel(**inputs):
    in_maps = _host_prep(inputs)
    nc = _build_nc()
    res = run_bass_kernel_spmd(nc, in_maps, core_ids=list(range(NCORES)))
    out = np.zeros((S, E), dtype=np.float32)
    for c in range(NCORES):
        out[c::8, :] = res.results[c]["y"]
    return out.reshape(1, S, E)
